# version 06 (final)
# baseline (speedup 1.0000x reference)
import numpy as np
import ml_dtypes
import concourse.bass as bass
import concourse.mybir as mybir
from concourse.bass_utils import run_bass_kernel_spmd

F32 = mybir.dt.float32
BF16 = mybir.dt.bfloat16
I8 = mybir.dt.int8
AX = mybir.AxisListType
OP = mybir.AluOpType
ACT = mybir.ActivationFunctionType
BF = ml_dtypes.bfloat16

NCORES = 8
T_OWN = 1024
HALO = 512
T_BLK = T_OWN + HALO
NT_B = T_BLK // 128
NT_O = T_OWN // 128
NHALO = HALO // 128
D = 2048
H = 8192
NDC = D // 128
NHT = H // 512
NHC = H // 128
QH, KVH, HD, WIN = 16, 4, 128, 512
WKEY = WIN + 128


def build_program():
    nc = bass.Bass("TRN2", target_bir_lowering=False, detect_race_conditions=False)
    dt = nc.dram_tensor

    x_in = dt("x_in", [T_BLK, D], F32, kind="ExternalInput").ap()
    wg_in = dt("wg_in", [NHT, D, 512], F32, kind="ExternalInput").ap()
    wv_in = dt("wv_in", [NHT, D, 512], F32, kind="ExternalInput").ap()
    wo_in = dt("wo_in", [4, NHC, 128, 512], F32, kind="ExternalInput").ap()
    wq_in = dt("wq_in", [NDC, 128, D], F32, kind="ExternalInput").ap()
    wk_in = dt("wk_in", [NDC, 128, 512], F32, kind="ExternalInput").ap()
    wvv_in = dt("wvv_in", [NDC, 128, 512], F32, kind="ExternalInput").ap()
    wom_in = dt("wom_in", [16, 128, D], F32, kind="ExternalInput").ap()
    gb_in = dt("gb_in", [1, H], F32, kind="ExternalInput").ap()
    vb_in = dt("vb_in", [1, H], F32, kind="ExternalInput").ap()
    ob_in = dt("ob_in", [1, D], F32, kind="ExternalInput").ap()
    mask_in = dt("mask_in", [NT_O, 128, WKEY], F32, kind="ExternalInput").ap()
    ctq_in = dt("ctq_in", [T_OWN, 512], F32, kind="ExternalInput").ap()
    stq_in = dt("stq_in", [T_OWN, 512], F32, kind="ExternalInput").ap()
    ctk_in = dt("ctk_in", [T_BLK, 256], F32, kind="ExternalInput").ap()
    stk_in = dt("stk_in", [T_BLK, 256], F32, kind="ExternalInput").ap()
    ident_in = dt("ident_in", [128, 128], BF16, kind="ExternalInput").ap()
    ones_in = dt("ones_in", [1, 128], F32, kind="ExternalInput").ap()
    out_hbm = dt("out", [T_OWN, D], F32, kind="ExternalOutput").ap()

    u_hbm = dt("u_hbm", [NT_B, NHT, 128, 512], F32).ap()
    q1_hbm = dt("q1_hbm", [T_BLK, D], BF16).ap()
    qu_hbm = dt("qu_hbm", [T_BLK, H], BF16).ap()
    h2_hbm = dt("h2_hbm", [T_BLK, D], BF16).ap()
    xw_hbm = dt("xw_hbm", [T_BLK, D], F32).ap()
    q_hbm = dt("q_hbm", [T_OWN, D], BF16).ap()
    k_hbm = dt("k_hbm", [T_BLK, 512], BF16).ap()
    p_hbm = [dt(f"p_hbm{i}", [T_OWN, WKEY], BF16).ap() for i in range(2)]
    gi8 = dt("gi8", [NHT, 128, H], I8).ap()
    ar_hbm = dt("ar_hbm", [NT_B, 128], BF16).ap()
    vi8 = dt("vi8", [NHT, 128, H], I8).ap()
    oi8 = dt("oi8", [4, NHC, 128, 512], I8).ap()

    import contextlib
    top = contextlib.ExitStack()
    with top:
        sbT = lambda n, s, d_: top.enter_context(nc.sbuf_tensor(n, s, d_))
        scal = sbT("scal", [128, 64], F32)
        gam = sbT("gam", [1, 16], F32)
        biasg = sbT("biasg", [1, H], BF16)
        biasv = sbT("biasv", [1, H], BF16)
        biaso = sbT("biaso", [1, D], BF16)
        ident = sbT("ident", [128, 128], BF16)
        ones_r = sbT("ones_r", [1, 128], F32)
        grow = sbT("grow", [1, T_BLK], BF16)
        vrow = sbT("vrow", [1, T_BLK], BF16)
        orow = sbT("orow", [1, T_BLK], BF16)
        arow_s = sbT("arow_s", [128, 1], BF16)
        arow_s2 = sbT("arow_s2", [1, 128], BF16)
        xts = [None]
        tmp1 = sbT("tmp1", [128, D], F32)

        ps_tiny = top.enter_context(nc.psum_tensor("ps_tiny", [1, 512], F32))
        ps_bc = top.enter_context(nc.psum_tensor("ps_bc", [128, 24], F32))
        bsc = sbT("bsc", [128, 12], BF16)
        r18 = sbT("r18", [1, 20], BF16)
        onesbf = sbT("onesbf", [1, 128], BF16)

        class Ctr:
            def __init__(self, name):
                self.sem = top.enter_context(nc.semaphore(name))
                self.v = 0

        sG = Ctr("sG"); sS = Ctr("sS"); sY = Ctr("sY")
        sV = Ctr("sV"); sA = Ctr("sA"); sM = Ctr("sM"); sB = Ctr("sB")
        sVC = Ctr("sVC"); sAC = Ctr("sAC")

        _CHAINOPS = {
            "tensor_copy", "tensor_tensor", "tensor_add", "tensor_sub", "tensor_mul",
            "tensor_max", "tensor_scalar", "tensor_single_scalar", "tensor_scalar_add",
            "tensor_scalar_sub", "tensor_scalar_mul", "tensor_scalar_max",
            "tensor_scalar_min", "tensor_reduce", "reduce_max", "reduce_sum",
            "scalar_tensor_tensor", "reciprocal", "memset", "memzero", "select",
            "activation", "mul", "add", "copy", "square", "sqrt", "sign",
        }

        class _Chain:
            def __init__(self, eng, ctr):
                self._e = eng
                self._c = ctr

            def __getattr__(self, name):
                attr = getattr(self._e, name)
                if name not in _CHAINOPS:
                    return attr
                ctr = self._c
                eng = self._e

                def call(*a, **k):
                    eng.wait_ge(ctr.sem, ctr.v)
                    inst = attr(*a, **k)
                    inst.then_inc(ctr.sem, 1)
                    ctr.v += 1
                    return inst
                return call

        blk_cm = nc.Block()
        blk = blk_cm.__enter__()

        def blkV(f):
            return blk.vector(lambda v, *a, **k: f(_Chain(v, sVC), *a, **k))

        def blkS(f):
            return blk.scalar(lambda s_, *a, **k: f(_Chain(s_, sAC), *a, **k))

        def barrier():
            sB.v += 5
            tgt = sB.v
            gv, sv_, yv = sG.v, sS.v, sY.v

            @blk.tensor
            def _(t, tgt=tgt):
                t.drain().then_inc(sB.sem, 1)
                t.wait_ge(sB.sem, tgt)

            @blk.vector
            def _(v, tgt=tgt):
                v.drain().then_inc(sB.sem, 1)
                v.wait_ge(sB.sem, tgt)

            @blk.scalar
            def _(s_, tgt=tgt, sv_=sv_):
                s_.wait_ge(sS.sem, sv_)
                s_.drain().then_inc(sB.sem, 1)
                s_.wait_ge(sB.sem, tgt)

            @blk.gpsimd
            def _(g, tgt=tgt, gv=gv):
                g.wait_ge(sG.sem, gv)
                g.drain().then_inc(sB.sem, 1)
                g.wait_ge(sB.sem, tgt)

            @blk.sync
            def _(y, tgt=tgt, yv=yv):
                y.wait_ge(sY.sem, yv)
                y.drain().then_inc(sB.sem, 1)
                y.wait_ge(sB.sem, tgt)

        @blk.gpsimd
        def _(g):
            g.dma_start(biasg[:], gb_in).then_inc(sG.sem, 16)
            g.dma_start(biasv[:], vb_in).then_inc(sG.sem, 16)
            g.dma_start(biaso[:], ob_in).then_inc(sG.sem, 16)
            g.dma_start(ident[:], ident_in).then_inc(sG.sem, 16)
            g.dma_start(ones_r[:], ones_in).then_inc(sG.sem, 16)
        sG.v += 80
        P0G = sG.v

        ph1 = contextlib.ExitStack()
        with ph1:
            stage = ph1.enter_context(nc.sbuf_tensor("stage1", [128, H], F32))
            xts[0] = ph1.enter_context(nc.sbuf_tensor("xt1", [128, 2 * D], F32))
            tbf = ph1.enter_context(nc.sbuf_tensor("tbf1", [128, D], BF16))
            qi8 = ph1.enter_context(nc.sbuf_tensor("qi81", [128, D], I8))

            slabs = []
            for i in range(NHT):
                for hh in range(2):
                    slabs.append(("g", wg_in[i, hh * 1024:(hh + 1) * 1024].rearrange(
                        "(c p) f -> p c f", p=128)))
            for i in range(NHT):
                for hh in range(2):
                    slabs.append(("v", wv_in[i, hh * 1024:(hh + 1) * 1024].rearrange(
                        "(c p) f -> p c f", p=128)))
            for i in range(4):
                for jg in range(8):
                    slabs.append(("o", wo_in[i, jg * 8:(jg + 1) * 8].rearrange("c p f -> p c f")))
            NSLAB = len(slabs)
            s_base, v_base = sS.v, sV.v

            @blkS
            def _(s):
                for k, (wn, src) in enumerate(slabs):
                    if k >= 2:
                        s.wait_ge(sV.sem, v_base + (k - 1))
                    dst = stage[:, (k % 2) * 4096:(k % 2) * 4096 + 4096].rearrange(
                        "p (c f) -> p c f", f=512)
                    s.dma_start(dst, src).then_inc(sS.sem, 16)
            sS.v += NSLAB * 16

            @blkV
            def _(v):
                v.memset(scal[:, 0:5], 0.0)
                v.memset(scal[:, 4:5], 1.0)
                v.memset(scal[:, 15:16], 1e-5)
                for k, (wn, src) in enumerate(slabs):
                    v.wait_ge(sS.sem, s_base + (k + 1) * 16)
                    col = {"g": 0, "v": 1, "o": 2}[wn]
                    v.tensor_reduce(scal[:, 3:4], stage[:, (k % 2) * 4096:(k % 2) * 4096 + 4096],
                                    AX.X, OP.add, apply_absolute_value=True)
                    v.tensor_add(scal[:, col:col + 1], scal[:, col:col + 1], scal[:, 3:4])
                    v.drain().then_inc(sV.sem, 1)
            sV.v += NSLAB
            P1V = sV.v

            @blkV
            def _(v):
                v.wait_ge(sV.sem, P1V)
                for col in range(3):
                    v.tensor_copy(bsc[:, col:col + 1], scal[:, col:col + 1])
                    v.tensor_copy(scal[:, 5:6], bsc[:, col:col + 1])
                    v.tensor_sub(scal[:, 6:7], scal[:, col:col + 1], scal[:, 5:6])
                    v.tensor_copy(bsc[:, 3 + col:4 + col], scal[:, 6:7])
                    v.tensor_copy(scal[:, 5:6], bsc[:, 3 + col:4 + col])
                    v.tensor_sub(scal[:, 7:8], scal[:, 6:7], scal[:, 5:6])
                    v.tensor_copy(bsc[:, 6 + col:7 + col], scal[:, 7:8])
                v.memset(bsc[:, 9:10], 1.0)
                v.drain().then_inc(sV.sem, 1)
            sV.v += 1
            P1Vb = sV.v

            @blk.tensor
            def _(t):
                t.wait_ge(sV.sem, P1Vb)
                t.matmul(ps_tiny[0:1, 0:9], bsc[:, 9:10], bsc[:, 0:9], start=True, stop=True)
                t.drain().then_inc(sM.sem, 1)
            sM.v += 1
            P1M = sM.v

            @blkV
            def _(v):
                v.wait_ge(sM.sem, P1M)
                v.wait_ge(sG.sem, P0G)
                v.tensor_copy(scal[0:1, 20:29], ps_tiny[0:1, 0:9])
                v.tensor_add(gam[0:1, 0:3], scal[0:1, 20:23], scal[0:1, 23:26])
                v.tensor_add(gam[0:1, 0:3], gam[0:1, 0:3], scal[0:1, 26:29])
                v.tensor_scalar(gam[0:1, 0:3], gam[0:1, 0:3], 1.0 / float(H * D), None, OP.mult)
                v.tensor_scalar(gam[0:1, 0:3], gam[0:1, 0:3], 1e-5, None, OP.max)
                v.tensor_scalar(gam[0:1, 3:6], gam[0:1, 0:3], 0.5, None, OP.mult)
                v.reciprocal(gam[0:1, 6:9], gam[0:1, 0:3])
                v.tensor_scalar(gam[0:1, 6:9], gam[0:1, 6:9], 254.0, None, OP.mult)
                v.reciprocal(gam[0:1, 9:12], gam[0:1, 6:9])
                v.tensor_scalar(gam[0:1, 11:12], gam[0:1, 11:12], 0.5, None, OP.mult)
                v.tensor_copy(gam[0:1, 12:15], gam[0:1, 3:6])
                v.tensor_copy(gam[0:1, 15:16], gam[0:1, 9:10])
                v.tensor_copy(r18[0:1, 0:3], gam[0:1, 3:6])
                v.tensor_copy(r18[0:1, 3:6], gam[0:1, 9:12])
                v.tensor_copy(onesbf[0:1, :], ones_r[0:1, :])
                v.drain().then_inc(sV.sem, 1)
            sV.v += 1
            P1V2 = sV.v

            @blkV
            def _(v):
                v.wait_ge(sV.sem, P1V2)
                v.tensor_copy(scal[0:1, 44:47], r18[0:1, 0:3])
                v.tensor_copy(scal[0:1, 47:50], r18[0:1, 3:6])
                v.tensor_sub(scal[0:1, 44:47], gam[0:1, 3:6], scal[0:1, 44:47])
                v.tensor_sub(scal[0:1, 47:50], gam[0:1, 9:12], scal[0:1, 47:50])
                v.tensor_copy(r18[0:1, 6:9], scal[0:1, 44:47])
                v.tensor_copy(r18[0:1, 9:12], scal[0:1, 47:50])
                v.tensor_copy(scal[0:1, 40:43], r18[0:1, 6:9])
                v.tensor_copy(scal[0:1, 36:39], r18[0:1, 9:12])
                v.tensor_sub(scal[0:1, 40:43], scal[0:1, 44:47], scal[0:1, 40:43])
                v.tensor_sub(scal[0:1, 36:39], scal[0:1, 47:50], scal[0:1, 36:39])
                v.tensor_copy(r18[0:1, 12:15], scal[0:1, 40:43])
                v.tensor_copy(r18[0:1, 15:18], scal[0:1, 36:39])
                v.drain().then_inc(sV.sem, 1)
            sV.v += 1
            P1V3 = sV.v

            @blk.tensor
            def _(t):
                t.wait_ge(sV.sem, P1V3)
                t.matmul(ps_bc[:, 0:18], onesbf[0:1, :], r18[0:1, 0:18], start=True, stop=True)
                t.drain().then_inc(sM.sem, 1)
            sM.v += 1
            P1M2 = sM.v

            @blkV
            def _(v):
                v.wait_ge(sM.sem, P1M2)
                v.tensor_copy(scal[:, 20:38], ps_bc[:, 0:18])
                v.tensor_add(scal[:, 20:26], scal[:, 20:26], scal[:, 26:32])
                v.tensor_add(scal[:, 20:26], scal[:, 20:26], scal[:, 32:38])
                v.tensor_copy(scal[:, 8:11], scal[:, 20:23])
                v.tensor_scalar(scal[:, 11:14], scal[:, 20:23], -1.0, None, OP.mult)
                v.tensor_copy(scal[:, 52:55], scal[:, 23:26])
                v.drain().then_inc(sV.sem, 1)
            sV.v += 1
            barrier()

            def emit_ln_quant(ntc, src_hbm, src_off):
                last_use = {0: sV.v, 1: sV.v}
                for i in range(ntc):
                    row = src_off + i
                    par = i % 2

                    @blk.gpsimd
                    def _(g, row=row, par=par, lu=last_use[par]):
                        g.wait_ge(sV.sem, lu)
                        g.dma_start(xts[0][:, par * D:par * D + D],
                                    src_hbm[row * 128:(row + 1) * 128, :]).then_inc(sG.sem, 16)
                    sG.v += 16
                    c_g = sG.v

                    @blkV
                    def _(v, par=par, c_g=c_g):
                        v.wait_ge(sG.sem, c_g)
                        xv = xts[0][:, par * D:par * D + D]
                        v.tensor_reduce(scal[:, 60:61], xv, AX.X, OP.add)
                        v.tensor_scalar(scal[:, 60:61], scal[:, 60:61], 1.0 / D, None, OP.mult)
                        v.tensor_scalar(scal[:, 61:62], scal[:, 60:61], -1.0, None, OP.mult)
                        v.drain().then_inc(sV.sem, 1)
                    sV.v += 1
                    c_v1 = sV.v

                    @blkS
                    def _(s, par=par, c_v1=c_v1):
                        s.wait_ge(sV.sem, c_v1)
                        xv = xts[0][:, par * D:par * D + D]
                        s.activation(tmp1[:], xv, ACT.Square, bias=scal[:, 61:62], scale=1.0,
                                     accum_out=scal[:, 62:63])
                        s.drain().then_inc(sA.sem, 1)
                    sA.v += 1
                    c_a1 = sA.v

                    @blkV
                    def _(v, c_a1=c_a1):
                        v.wait_ge(sA.sem, c_a1)
                        v.tensor_reduce(scal[:, 63:64], tmp1[:], AX.X, OP.max)
                        v.drain().then_inc(sV.sem, 1)
                    sV.v += 1
                    c_v2 = sV.v

                    @blkS
                    def _(s, c_v2=c_v2):
                        s.wait_ge(sV.sem, c_v2)
                        s.activation(scal[:, 55:56], scal[:, 62:63], ACT.Sqrt, bias=scal[:, 15:16],
                                     scale=1.0 / D)
                        s.activation(scal[:, 57:58], scal[:, 63:64], ACT.Sqrt, bias=0.0, scale=1.0)
                        s.drain().then_inc(sA.sem, 1)
                    sA.v += 1
                    c_a2 = sA.v

                    @blkV
                    def _(v, i=i, par=par, c_a2=c_a2):
                        v.wait_ge(sA.sem, c_a2)
                        xv = xts[0][:, par * D:par * D + D]
                        v.reciprocal(scal[:, 56:57], scal[:, 55:56])
                        v.tensor_mul(scal[:, 58:59], scal[:, 57:58], scal[:, 56:57])
                        v.tensor_scalar(scal[:, 58:59], scal[:, 58:59], 1e-5, None, OP.max)
                        v.reciprocal(scal[:, 59:60], scal[:, 58:59])
                        v.tensor_scalar(scal[:, 59:60], scal[:, 59:60], 127.0, None, OP.mult)
                        v.tensor_mul(scal[:, 59:60], scal[:, 59:60], scal[:, 56:57])
                        v.tensor_scalar(tmp1[:], xv, scal[:, 60:61], scal[:, 59:60],
                                        OP.subtract, OP.mult)
                        v.tensor_copy(qi8[:], tmp1[:])
                        v.tensor_mul(scal[:, 16 + i:17 + i], scal[:, 58:59], scal[:, 52:53])
                        v.tensor_mul(scal[:, 28 + i:29 + i], scal[:, 58:59], scal[:, 53:54])
                        v.reciprocal(scal[:, 40 + i:41 + i], scal[:, 58:59])
                        v.drain().then_inc(sV.sem, 1)
                    sV.v += 1
                    last_use[par] = sV.v
                    c_v3 = sV.v

                    @blkS
                    def _(s, c_v3=c_v3, wg=sG.v):
                        s.wait_ge(sV.sem, c_v3)
                        s.wait_ge(sG.sem, wg)
                        s.activation(tbf[:], qi8[:], ACT.Copy, bias=0.0, scale=1.0)
                        s.drain().then_inc(sA.sem, 1)
                    sA.v += 1
                    c_a3 = sA.v

                    @blk.gpsimd
                    def _(g, i=i, c_a3=c_a3):
                        g.wait_ge(sA.sem, c_a3)
                        g.dma_start(q1_hbm[i * 128:(i + 1) * 128, :], tbf[:]).then_inc(sG.sem, 16)
                    sG.v += 16

            def emit_alpha_rows(ntc):
                for i in range(ntc):
                    @blk.gpsimd
                    def _(g, i=i):
                        g.dma_start(ar_hbm[i:i + 1, :], scal[:, 40 + i:41 + i]).then_inc(sG.sem, 16)
                    sG.v += 16
                    cw = sG.v

                    @blk.gpsimd
                    def _(g, i=i, cw=cw):
                        g.wait_ge(sG.sem, cw)
                        g.dma_start(arow_s[0:1, 0:128].rearrange("a b -> a b") if False else arow_s2[0:1, :],
                                    ar_hbm[i:i + 1, :]).then_inc(sG.sem, 16)
                    sG.v += 16
                    cr = sG.v

                    @blkV
                    def _(v, i=i, cr=cr):
                        v.wait_ge(sG.sem, cr)
                        v.tensor_scalar(grow[0:1, i * 128:(i + 1) * 128], arow_s2[0:1, :],
                                        gam[0:1, 6:7], None, OP.mult)
                        v.tensor_scalar(vrow[0:1, i * 128:(i + 1) * 128], arow_s2[0:1, :],
                                        gam[0:1, 7:8], None, OP.mult)
                        v.drain().then_inc(sV.sem, 1)
                    sV.v += 1

            emit_ln_quant(NT_B, x_in, 0)
            emit_alpha_rows(NT_B)
            barrier()

        def emit_ffn_gv(ntc, int8src, ph, tag):
            sb3 = lambda n, s, d_: ph.enter_context(nc.sbuf_tensor(n + tag, s, d_))
            stage = sb3("stage3", [128, 4096], F32)
            s2a = sb3("s2a3", [128, 4096], BF16)
            wgt = [sb3(f"wgt{p}", [128, H], BF16) for p in range(2)]
            wvt = [sb3(f"wvt{p}", [128, H], BF16) for p in range(2)]
            gatev = sb3("gatev", [128, 2048], F32)
            uf32 = sb3("uf32", [128, 1024], F32)
            ps_g = [ph.enter_context(nc.psum_tensor(f"ps_g{k}" + tag, [128, 512], F32)) for k in range(2)]
            ps_v = [ph.enter_context(nc.psum_tensor(f"ps_v{k}" + tag, [128, 512], F32)) for k in range(2)]
            TQ = ntc * 128
            q1f = sb3("q1f", [128, NDC * TQ], BF16)

            for d_ in range(NDC):
                @blk.sync
                def _(y, d_=d_):
                    y.dma_start_transpose(q1f[:, d_ * TQ:(d_ + 1) * TQ],
                                          q1_hbm[0:TQ, d_ * 128:(d_ + 1) * 128]).then_inc(sY.sem, 16)
                sY.v += 16
            QF_Y = sY.v

            mm_hist = []
            cache_hist = []
            ev_done = [sA.v]
            umul_hist = []
            uw_hist = []
            grp = 0
            for ht in range(NHT):
                par = ht % 2
                free_mm = mm_hist[ht - 1] if ht >= 1 else 0
                free_cache = cache_hist[ht - 1] if ht >= 1 else 0
                if not int8src:
                    for wname, win_, wbuf, i8dst, nacol, pacol in (
                        ("g", wg_in, wgt[par], gi8, 11, 8),
                        ("v", wv_in, wvt[par], vi8, 12, 9),
                    ):
                        for hh in range(2):
                            @blkS
                            def _(s, ht=ht, win_=win_, hh=hh, fm=free_mm, pv=sV.v):
                                if fm:
                                    s.wait_ge(sM.sem, fm)
                                s.wait_ge(sV.sem, pv)
                                s.dma_start(stage.rearrange("p (c f) -> p c f", f=512),
                                            win_[ht, hh * 1024:(hh + 1) * 1024].rearrange(
                                                "(c p) f -> p c f", p=128)).then_inc(sS.sem, 16)
                            sS.v += 16
                            c_sd = sS.v

                            @blkS
                            def _(s, c_sd=c_sd, nacol=nacol, pacol=pacol, wbuf=wbuf, hh=hh,
                                  fc=free_cache, pv=sV.v):
                                s.wait_ge(sS.sem, c_sd)
                                s.wait_ge(sV.sem, pv)
                                if fc:
                                    s.wait_ge(sG.sem, fc)
                                s.activation(s2a[:], stage[:], ACT.Sign,
                                             bias=scal[:, nacol:nacol + 1], scale=1.0)
                                s.activation(wbuf[:, hh * 4096:(hh + 1) * 4096], stage[:],
                                             ACT.Sign, bias=scal[:, pacol:pacol + 1], scale=1.0)
                                s.drain().then_inc(sA.sem, 1)
                            sA.v += 1
                            c_aq = sA.v

                            @blkV
                            def _(v, c_aq=c_aq, wbuf=wbuf, hh=hh):
                                v.wait_ge(sA.sem, c_aq)
                                v.tensor_add(wbuf[:, hh * 4096:(hh + 1) * 4096],
                                             wbuf[:, hh * 4096:(hh + 1) * 4096], s2a[:])
                                v.drain().then_inc(sV.sem, 1)
                            sV.v += 1
                            c_vq = sV.v

                        @blk.gpsimd
                        def _(g, ht=ht, c_vq=sV.v, wbuf=wbuf, i8dst=i8dst):
                            g.wait_ge(sV.sem, c_vq)
                            g.dma_start(i8dst[ht], wbuf[:]).then_inc(sG.sem, 16)
                        sG.v += 16
                    cache_hist.append(sG.v)
                    w_ready = ("v", sV.v)
                else:
                    @blk.gpsimd
                    def _(g, ht=ht, par=par, fm=free_mm):
                        if fm:
                            g.wait_ge(sM.sem, fm)
                        g.dma_start(wgt[par][:], gi8[ht]).then_inc(sG.sem, 16)
                        g.dma_start(wvt[par][:], vi8[ht]).then_inc(sG.sem, 16)
                    sG.v += 32
                    cache_hist.append(sG.v)
                    w_ready = ("g", sG.v)

                for tc in range(ntc):
                    pp = tc % 2

                    @blk.tensor
                    def _(t, ht=ht, tc=tc, par=par, pp=pp, wr=w_ready, grp=grp,
                          ev=ev_done[0], qy=QF_Y, TQ=TQ):
                        if ht == 0 and tc == 0:
                            t.wait_ge(sY.sem, qy)
                        if tc == 0:
                            t.wait_ge(sV.sem, wr[1]) if wr[0] == "v" else t.wait_ge(sG.sem, wr[1])
                        if grp >= 1:
                            t.wait_ge(sA.sem, ev)
                        for d_ in range(NDC):
                            t.matmul(ps_g[pp][:],
                                     q1f[:, d_ * TQ + tc * 128:d_ * TQ + tc * 128 + 128],
                                     wgt[par][:, d_ * 512:(d_ + 1) * 512],
                                     start=(d_ == 0), stop=False)
                        t.matmul(ps_g[pp][:], grow[0:1, tc * 128:(tc + 1) * 128],
                                 biasg[0:1, ht * 512:(ht + 1) * 512], start=False, stop=True)
                        for d_ in range(NDC):
                            t.matmul(ps_v[pp][:],
                                     q1f[:, d_ * TQ + tc * 128:d_ * TQ + tc * 128 + 128],
                                     wvt[par][:, d_ * 512:(d_ + 1) * 512],
                                     start=(d_ == 0), stop=False)
                        t.matmul(ps_v[pp][:], vrow[0:1, tc * 128:(tc + 1) * 128],
                                 biasv[0:1, ht * 512:(ht + 1) * 512], start=False, stop=True).then_inc(sM.sem, 1)
                    sM.v += 1
                    c_m = sM.v

                    gpar = grp % 2

                    @blkS
                    def _(s, tc=tc, pp=pp, gpar=gpar, c_m=c_m,
                          um=(umul_hist[grp - 1] if grp >= 1 else 0)):
                        s.wait_ge(sM.sem, c_m)
                        if um:
                            s.wait_ge(sV.sem, um)
                        s.activation(gatev[:, gpar * 1024:gpar * 1024 + 512], ps_g[pp][:],
                                     ACT.Silu, bias=0.0, scale=scal[:, 16 + tc:17 + tc])
                        s.activation(gatev[:, gpar * 1024 + 512:gpar * 1024 + 1024], ps_v[pp][:],
                                     ACT.Copy, bias=0.0, scale=scal[:, 28 + tc:29 + tc])
                        s.drain().then_inc(sA.sem, 1)
                    sA.v += 1
                    ev_done[0] = sA.v
                    c_a = sA.v

                    @blkV
                    def _(v, gpar=gpar, c_a=c_a,
                          uw=(uw_hist[grp - 1] if grp >= 1 else 0)):
                        v.wait_ge(sA.sem, c_a)
                        if uw:
                            v.wait_ge(sG.sem, uw)
                        v.tensor_mul(uf32[:, gpar * 512:gpar * 512 + 512],
                                     gatev[:, gpar * 1024:gpar * 1024 + 512],
                                     gatev[:, gpar * 1024 + 512:gpar * 1024 + 1024])
                        v.drain().then_inc(sV.sem, 1)
                    sV.v += 1
                    umul_hist.append(sV.v)
                    c_v = sV.v

                    @blk.gpsimd
                    def _(g, ht=ht, tc=tc, gpar=gpar, c_v=c_v):
                        g.wait_ge(sV.sem, c_v)
                        g.dma_start(u_hbm[tc, ht],
                                    uf32[:, gpar * 512:gpar * 512 + 512]).then_inc(sG.sem, 16)
                    sG.v += 16
                    uw_hist.append(sG.v)
                    grp += 1
                mm_hist.append(sM.v)

        ph3 = contextlib.ExitStack()
        with ph3:
            emit_ffn_gv(NT_B, int8src=False, ph=ph3, tag='A')
            barrier()

        def emit_ffn3(ntc, int8src, ph, x_src, x_dst, tag):
            sb4 = lambda n, s, d_: ph.enter_context(nc.sbuf_tensor(n + tag, s, d_))
            stage = sb4("stage4", [128, H], F32)
            s2a = sb4("s2a4", [128, 512], BF16)
            tbf = sb4("tbf4", [128, H], BF16)
            qi8 = sb4("qi84", [128, H], I8)
            wot = [sb4(f"wot{p}", [128, 512], BF16) for p in range(2)]
            xts[0] = ph.enter_context(nc.sbuf_tensor("xt4" + tag, [128, 2 * D], F32))
            stb = sb4("stb", [128, 2 * 512], BF16)
            ps_o = [ph.enter_context(nc.psum_tensor(f"ps_o{k}" + tag, [128, 512], F32)) for k in range(4)]

            for tc in range(ntc):
                @blk.gpsimd
                def _(g, tc=tc, pv=sV.v):
                    g.wait_ge(sV.sem, pv)
                    g.dma_start(stage.rearrange("p (h f) -> p h f", f=512), u_hbm[tc].rearrange("h p f -> p h f")).then_inc(sG.sem, 16)
                sG.v += 16
                c_g = sG.v

                @blkV
                def _(v, tc=tc, c_g=c_g, pa=sA.v):
                    v.wait_ge(sG.sem, c_g)
                    v.wait_ge(sA.sem, pa)
                    v.tensor_reduce(scal[:, 60:61], stage[:], AX.X, OP.max,
                                    apply_absolute_value=True)
                    v.tensor_scalar(scal[:, 60:61], scal[:, 60:61], 1e-5, None, OP.max)
                    v.reciprocal(scal[:, 61:62], scal[:, 60:61])
                    v.tensor_scalar(scal[:, 61:62], scal[:, 61:62], 127.0, None, OP.mult)
                    v.tensor_mul(scal[:, 16 + tc:17 + tc], scal[:, 60:61], scal[:, 54:55])
                    v.reciprocal(scal[:, 40 + tc:41 + tc], scal[:, 60:61])
                    v.tensor_scalar(stage[:], stage[:], scal[:, 61:62], None, OP.mult)
                    v.tensor_copy(qi8[:], stage[:])
                    v.drain().then_inc(sV.sem, 1)
                sV.v += 1
                c_v = sV.v

                @blk.gpsimd
                def _(g, tc=tc, c_v=c_v):
                    g.wait_ge(sV.sem, c_v)
                    g.dma_start(ar_hbm[tc:tc + 1, :], scal[:, 40 + tc:41 + tc]).then_inc(sG.sem, 16)
                sG.v += 16
                cw = sG.v

                @blk.gpsimd
                def _(g, tc=tc, cw=cw, pv=sV.v):
                    g.wait_ge(sG.sem, cw)
                    g.wait_ge(sV.sem, pv)
                    g.dma_start(arow_s2[0:1, :], ar_hbm[tc:tc + 1, :]).then_inc(sG.sem, 16)
                sG.v += 16
                cr = sG.v

                @blkV
                def _(v, tc=tc, cr=cr):
                    v.wait_ge(sG.sem, cr)
                    v.tensor_scalar(orow[0:1, tc * 128:(tc + 1) * 128], arow_s2[0:1, :],
                                    gam[0:1, 8:9], None, OP.mult)
                    v.drain().then_inc(sV.sem, 1)
                sV.v += 1
                c_v2 = sV.v

                @blkS
                def _(s, tc=tc, c_v2=c_v2, wg=sG.v):
                    s.wait_ge(sV.sem, c_v2)
                    s.wait_ge(sG.sem, wg)
                    s.activation(tbf[:], qi8[:], ACT.Copy, bias=0.0, scale=1.0)
                    s.drain().then_inc(sA.sem, 1)
                sA.v += 1
                c_a = c_v2 * 0 + sA.v

                @blk.gpsimd
                def _(g, tc=tc, c_a=c_a):
                    g.wait_ge(sA.sem, c_a)
                    g.dma_start(qu_hbm[tc * 128:(tc + 1) * 128, :], tbf[:]).then_inc(sG.sem, 16)
                sG.v += 16
            barrier()

            tau_blocks = [list(range(i, min(i + 4, ntc))) for i in range(0, ntc, 4)]
            ev3 = [sV.v]
            gw3 = [0]
            o8w = []
            hc_mm = []
            for dtq in range(4):
                for tb in tau_blocks:
                    @blk.gpsimd
                    def _(g, dtq=dtq, tb=tb, x_src=x_src, pv=ev3[0], wg=gw3[0]):
                        g.wait_ge(sV.sem, pv)
                        if wg:
                            g.wait_ge(sG.sem, wg)
                        for j, tc in enumerate(tb):
                            g.dma_start(xts[0][:, j * 512:(j + 1) * 512],
                                        x_src[tc * 128:(tc + 1) * 128,
                                              dtq * 512:(dtq + 1) * 512]).then_inc(sG.sem, 16)
                    sG.v += 16 * len(tb)
                    c_xld = sG.v

                    for hc in range(NHC):
                        par = hc % 2
                        kidx = len(hc_mm)
                        free_mm = hc_mm[kidx - 1] if kidx >= 1 else 0
                        if not int8src:
                            @blkS
                            def _(s, dtq=dtq, hc=hc, fm=free_mm):
                                if fm:
                                    s.wait_ge(sM.sem, fm)
                                s.dma_start(stage[:, 0:512], wo_in[dtq, hc]).then_inc(sS.sem, 16)
                            sS.v += 16
                            c_sd = sS.v

                            @blkS
                            def _(s, c_sd=c_sd, par=par, pv=sV.v,
                                  fc=(o8w[len(o8w) - 1] if len(o8w) >= 1 else 0)):
                                s.wait_ge(sS.sem, c_sd)
                                s.wait_ge(sV.sem, pv)
                                if fc:
                                    s.wait_ge(sG.sem, fc)
                                s.activation(s2a[:], stage[:, 0:512], ACT.Sign,
                                             bias=scal[:, 13:14], scale=1.0)
                                s.activation(wot[par][:], stage[:, 0:512], ACT.Sign,
                                             bias=scal[:, 10:11], scale=1.0)
                                s.drain().then_inc(sA.sem, 1)
                            sA.v += 1
                            c_aq = sA.v

                            @blkV
                            def _(v, c_aq=c_aq, par=par):
                                v.wait_ge(sA.sem, c_aq)
                                v.tensor_add(wot[par][:], wot[par][:], s2a[:])
                                v.drain().then_inc(sV.sem, 1)
                            sV.v += 1
                            c_vq = sV.v

                            @blk.gpsimd
                            def _(g, dtq=dtq, hc=hc, c_vq=c_vq, par=par):
                                g.wait_ge(sV.sem, c_vq)
                                g.dma_start(oi8[dtq, hc], wot[par][:]).then_inc(sG.sem, 16)
                            sG.v += 16
                            o8w.append(sG.v)
                            w_ready = ("v", c_vq)
                        else:
                            @blk.gpsimd
                            def _(g, dtq=dtq, hc=hc, par=par, fm=free_mm):
                                if fm:
                                    g.wait_ge(sM.sem, fm)
                                g.dma_start(wot[par][:], oi8[dtq, hc]).then_inc(sG.sem, 16)
                            sG.v += 16
                            w_ready = ("g", sG.v)

                        @blk.sync
                        def _(y, hc=hc, tb=tb, par=par, fm=free_mm):
                            if fm:
                                y.wait_ge(sM.sem, fm)
                            y.dma_start_transpose(
                                stb[:, par * 512:par * 512 + len(tb) * 128],
                                qu_hbm[tb[0] * 128:(tb[0] + len(tb)) * 128,
                                       hc * 128:(hc + 1) * 128]).then_inc(sY.sem, 16)
                        sY.v += 16
                        c_y = sY.v

                        @blk.tensor
                        def _(t, dtq=dtq, hc=hc, tb=tb, wr=w_ready, c_y=c_y, par=par,
                              ev=ev3[0]):
                            t.wait_ge(sY.sem, c_y)
                            t.wait_ge(sV.sem, wr[1]) if wr[0] == "v" else t.wait_ge(sG.sem, wr[1])
                            if hc == 0:
                                t.wait_ge(sV.sem, ev)
                            last = (hc == NHC - 1)
                            inst = None
                            for j, tc in enumerate(tb):
                                inst = t.matmul(ps_o[j][:], stb[:, par * 512 + j * 128:par * 512 + (j + 1) * 128],
                                                wot[par][:], start=(hc == 0), stop=False)
                            if last:
                                for j, tc in enumerate(tb):
                                    inst = t.matmul(ps_o[j][:],
                                                    orow[0:1, tc * 128:(tc + 1) * 128],
                                                    biaso[0:1, dtq * 512:(dtq + 1) * 512], start=False, stop=True)
                            inst.then_inc(sM.sem, 1)
                        sM.v += 1
                        hc_mm.append(sM.v)
                    c_m = sM.v

                    @blkV
                    def _(v, dtq=dtq, tb=tb, c_m=c_m, c_xld=c_xld):
                        v.wait_ge(sM.sem, c_m)
                        v.wait_ge(sG.sem, c_xld)
                        for j, tc in enumerate(tb):
                            v.scalar_tensor_tensor(xts[0][:, j * 512:(j + 1) * 512], ps_o[j][:],
                                                   scal[:, 16 + tc:17 + tc],
                                                   xts[0][:, j * 512:(j + 1) * 512],
                                                   OP.mult, OP.add)
                        v.drain().then_inc(sV.sem, 1)
                    sV.v += 1
                    ev3[0] = sV.v
                    c_v = sV.v

                    @blk.gpsimd
                    def _(g, dtq=dtq, tb=tb, c_v=c_v, x_dst=x_dst):
                        g.wait_ge(sV.sem, c_v)
                        for j, tc in enumerate(tb):
                            g.dma_start(x_dst[tc * 128:(tc + 1) * 128, dtq * 512:(dtq + 1) * 512],
                                        xts[0][:, j * 512:(j + 1) * 512]).then_inc(sG.sem, 16)
                    sG.v += 16 * len(tb)
                    gw3[0] = sG.v

        ph4 = contextlib.ExitStack()
        with ph4:
            emit_ffn3(NT_B, int8src=False, ph=ph4, x_src=x_in, x_dst=xw_hbm, tag='A')
            barrier()

        ph5 = contextlib.ExitStack()
        with ph5:
            tbf5 = ph5.enter_context(nc.sbuf_tensor("tbf5", [128, D], BF16))
            xts[0] = ph5.enter_context(nc.sbuf_tensor("xt5", [128, 2 * D], F32))
            lu5 = {0: sV.v, 1: sV.v}
            for tc in range(NT_B):
                par = tc % 2

                @blk.gpsimd
                def _(g, tc=tc, par=par, lu=lu5[par]):
                    g.wait_ge(sV.sem, lu)
                    g.dma_start(xts[0][:, par * D:par * D + D],
                                xw_hbm[tc * 128:(tc + 1) * 128, :]).then_inc(sG.sem, 16)
                sG.v += 16
                c_g = sG.v

                @blkV
                def _(v, par=par, c_g=c_g):
                    v.wait_ge(sG.sem, c_g)
                    xv = xts[0][:, par * D:par * D + D]
                    v.tensor_reduce(scal[:, 60:61], xv, AX.X, OP.add)
                    v.tensor_scalar(scal[:, 60:61], scal[:, 60:61], 1.0 / D, None, OP.mult)
                    v.tensor_scalar(scal[:, 61:62], scal[:, 60:61], -1.0, None, OP.mult)
                    v.drain().then_inc(sV.sem, 1)
                sV.v += 1
                c_v1 = sV.v

                @blkS
                def _(s, par=par, c_v1=c_v1):
                    s.wait_ge(sV.sem, c_v1)
                    xv = xts[0][:, par * D:par * D + D]
                    s.activation(tmp1[:], xv, ACT.Square, bias=scal[:, 61:62], scale=1.0,
                                 accum_out=scal[:, 62:63])
                    s.activation(scal[:, 55:56], scal[:, 62:63], ACT.Sqrt, bias=scal[:, 15:16],
                                 scale=1.0 / D)
                    s.drain().then_inc(sA.sem, 1)
                sA.v += 1
                c_a1 = sA.v

                @blkV
                def _(v, par=par, c_a1=c_a1, wg=sG.v):
                    v.wait_ge(sA.sem, c_a1)
                    v.wait_ge(sG.sem, wg)
                    xv = xts[0][:, par * D:par * D + D]
                    v.reciprocal(scal[:, 56:57], scal[:, 55:56])
                    v.tensor_scalar(tbf5[:], xv, scal[:, 60:61], scal[:, 56:57],
                                    OP.subtract, OP.mult)
                    v.drain().then_inc(sV.sem, 1)
                sV.v += 1
                lu5[par] = sV.v
                c_v2 = sV.v

                @blk.gpsimd
                def _(g, tc=tc, c_v2=c_v2):
                    g.wait_ge(sV.sem, c_v2)
                    g.dma_start(h2_hbm[tc * 128:(tc + 1) * 128, :], tbf5[:]).then_inc(sG.sem, 16)
                sG.v += 16
            barrier()

        pha = contextlib.ExitStack()
        with pha:
            v_tm = pha.enter_context(nc.sbuf_tensor("v_tm", [128, NT_B * 512], BF16))

            ph6 = contextlib.ExitStack()
            with ph6:
                sb6 = lambda n, s, d_: ph6.enter_context(nc.sbuf_tensor(n, s, d_))
                h2f = sb6("h2f", [128, NDC * T_BLK], BF16)
                wqm = sb6("wqm", [128, NDC * 1024], BF16)
                wkm = sb6("wkm", [128, NDC * 512], BF16)
                wvm = sb6("wvm", [128, NDC * 512], BF16)
                qraw = sb6("qraw", [128, 1024], F32)
                kraw = sb6("kraw", [128, 512], F32)
                qbf = sb6("qbf", [128, 1024], BF16)
                kbf = sb6("kbf", [128, 512], BF16)
                ropet = sb6("ropet", [128, 1536], F32)
                ps_q = ph6.enter_context(nc.psum_tensor("ps_q", [128, 1024], F32))
                ps_k = ph6.enter_context(nc.psum_tensor("ps_k", [128, 512], F32))
                ps_v6 = ph6.enter_context(nc.psum_tensor("ps_v6", [128, 512], F32))

                for d_ in range(NDC):
                    @blk.sync
                    def _(y, d_=d_):
                        y.dma_start_transpose(h2f[:, d_ * T_BLK:(d_ + 1) * T_BLK],
                                              h2_hbm[0:T_BLK, d_ * 128:(d_ + 1) * 128]).then_inc(sY.sem, 16)
                    sY.v += 16
                H2F_Y = sY.v

                evp = [sA.v]
                for half in range(2):
                    @blk.gpsimd
                    def _(g, half=half):
                        for d_ in range(NDC):
                            g.dma_start(wqm[:, d_ * 1024:(d_ + 1) * 1024],
                                        wq_in[d_, :, half * 1024:half * 1024 + 1024]).then_inc(sG.sem, 16)
                        if half == 0:
                            for d_ in range(NDC):
                                g.dma_start(wkm[:, d_ * 512:(d_ + 1) * 512], wk_in[d_]).then_inc(sG.sem, 16)
                                g.dma_start(wvm[:, d_ * 512:(d_ + 1) * 512], wvv_in[d_]).then_inc(sG.sem, 16)
                    sG.v += 16 * (NDC * 3 if half == 0 else NDC)
                    c_w = sG.v

                    for tc in range(NT_B):
                        own = tc >= NHALO
                        to = tc - NHALO

                        @blk.gpsimd
                        def _(g, tc=tc, own=own, to=to, half=half, pv=sV.v):
                            g.wait_ge(sV.sem, pv)
                            if own:
                                g.dma_start(ropet[:, 0:512],
                                            ctq_in[to * 128:(to + 1) * 128, :]).then_inc(sG.sem, 16)
                                g.dma_start(ropet[:, 512:1024],
                                            stq_in[to * 128:(to + 1) * 128, :]).then_inc(sG.sem, 16)
                            if half == 0:
                                g.dma_start(ropet[:, 1024:1280],
                                            ctk_in[tc * 128:(tc + 1) * 128, :]).then_inc(sG.sem, 16)
                                g.dma_start(ropet[:, 1280:1536],
                                            stk_in[tc * 128:(tc + 1) * 128, :]).then_inc(sG.sem, 16)
                        sG.v += 16 * ((2 if own else 0) + (2 if half == 0 else 0))
                        c_rt = sG.v

                        @blk.tensor
                        def _(t, tc=tc, own=own, half=half, c_w=c_w, hy=H2F_Y, ev=evp[0]):
                            if tc == 0:
                                t.wait_ge(sY.sem, hy)
                                t.wait_ge(sG.sem, c_w)
                            t.wait_ge(sA.sem, ev)
                            for d_ in range(NDC):
                                st = h2f[:, d_ * T_BLK + tc * 128:d_ * T_BLK + tc * 128 + 128]
                                if own:
                                    t.matmul(ps_q[:, 0:512], st, wqm[:, d_ * 1024:d_ * 1024 + 512],
                                             start=(d_ == 0), stop=(d_ == NDC - 1))
                                    t.matmul(ps_q[:, 512:1024], st,
                                             wqm[:, d_ * 1024 + 512:d_ * 1024 + 1024],
                                             start=(d_ == 0), stop=(d_ == NDC - 1))
                                if half == 0:
                                    t.matmul(ps_k[:], st, wkm[:, d_ * 512:(d_ + 1) * 512],
                                             start=(d_ == 0), stop=(d_ == NDC - 1))
                                    t.matmul(ps_v6[:], st, wvm[:, d_ * 512:(d_ + 1) * 512],
                                             start=(d_ == 0), stop=(d_ == NDC - 1))
                            t.drain().then_inc(sM.sem, 1)
                        sM.v += 1
                        c_m = sM.v

                        @blkS
                        def _(s, tc=tc, own=own, half=half, c_m=c_m, pv=sV.v):
                            s.wait_ge(sM.sem, c_m)
                            s.wait_ge(sV.sem, pv)
                            if own:
                                s.activation(qraw[:], ps_q[:], ACT.Copy, bias=0.0, scale=1.0)
                            if half == 0:
                                s.activation(kraw[:], ps_k[:], ACT.Copy, bias=0.0, scale=1.0)
                                s.activation(v_tm[:, tc * 512:(tc + 1) * 512], ps_v6[:],
                                             ACT.Copy, bias=0.0, scale=1.0)
                            s.drain().then_inc(sA.sem, 1)
                        sA.v += 1
                        evp[0] = sA.v
                        c_a = sA.v

                        @blkV
                        def _(v, own=own, half=half, c_a=c_a, c_rt=c_rt, wq=sG.v):
                            v.wait_ge(sA.sem, c_a)
                            v.wait_ge(sG.sem, c_rt)
                            v.wait_ge(sG.sem, wq)
                            if own:
                                c_ap = ropet[:, 0:512].rearrange("p (h f) -> p h f", f=64)
                                s_ap = ropet[:, 512:1024].rearrange("p (h f) -> p h f", f=64)
                                x1 = qraw.rearrange("p (h q) -> p h q", q=128)[:, :, 0:64]
                                x2 = qraw.rearrange("p (h q) -> p h q", q=128)[:, :, 64:128]
                                o1 = qbf.rearrange("p (h q) -> p h q", q=128)[:, :, 0:64]
                                o2 = qbf.rearrange("p (h q) -> p h q", q=128)[:, :, 64:128]
                                t1 = tmp1.rearrange("p (h q) -> p h q", q=128)[:, 0:8, 0:64]
                                t2 = tmp1.rearrange("p (h q) -> p h q", q=128)[:, 0:8, 64:128]
                                v.tensor_tensor(t1, x1, c_ap, OP.mult)
                                v.tensor_tensor(t2, x2, s_ap, OP.mult)
                                v.tensor_tensor(o1, t1, t2, OP.subtract)
                                v.tensor_tensor(t1, x1, s_ap, OP.mult)
                                v.tensor_tensor(t2, x2, c_ap, OP.mult)
                                v.tensor_tensor(o2, t1, t2, OP.add)
                            if half == 0:
                                ck = ropet[:, 1024:1280].rearrange("p (h f) -> p h f", f=64)
                                sk = ropet[:, 1280:1536].rearrange("p (h f) -> p h f", f=64)
                                y1 = kraw.rearrange("p (h q) -> p h q", q=128)[:, :, 0:64]
                                y2 = kraw.rearrange("p (h q) -> p h q", q=128)[:, :, 64:128]
                                ko1 = kbf.rearrange("p (h q) -> p h q", q=128)[:, :, 0:64]
                                ko2 = kbf.rearrange("p (h q) -> p h q", q=128)[:, :, 64:128]
                                u1 = tmp1.rearrange("p (h q) -> p h q", q=128)[:, 8:12, 0:64]
                                u2 = tmp1.rearrange("p (h q) -> p h q", q=128)[:, 8:12, 64:128]
                                v.tensor_tensor(u1, y1, ck, OP.mult)
                                v.tensor_tensor(u2, y2, sk, OP.mult)
                                v.tensor_tensor(ko1, u1, u2, OP.subtract)
                                v.tensor_tensor(u1, y1, sk, OP.mult)
                                v.tensor_tensor(u2, y2, ck, OP.mult)
                                v.tensor_tensor(ko2, u1, u2, OP.add)
                            v.drain().then_inc(sV.sem, 1)
                        sV.v += 1
                        c_v = sV.v

                        @blk.gpsimd
                        def _(g, tc=tc, own=own, to=to, half=half, c_v=c_v):
                            g.wait_ge(sV.sem, c_v)
                            if own:
                                g.dma_start(q_hbm[to * 128:(to + 1) * 128,
                                                  half * 1024:half * 1024 + 1024],
                                            qbf[:]).then_inc(sG.sem, 16)
                            if half == 0:
                                g.dma_start(k_hbm[tc * 128:(tc + 1) * 128, :], kbf[:]).then_inc(sG.sem, 16)
                        sG.v += 16 * ((1 if own else 0) + (1 if half == 0 else 0))
                barrier()

            aout_cm = nc.sbuf_tensor("aout", [128, QH * T_OWN], BF16)
            aout = aout_cm.__enter__()
            ph7 = contextlib.ExitStack()
            with ph7:
                sb7 = lambda n, s, d_: ph7.enter_context(nc.sbuf_tensor(n, s, d_))
                qf = sb7("qf", [128, QH * T_OWN], BF16)
                kf = sb7("kf", [128, KVH * T_BLK], BF16)
                masks = sb7("masks", [128, NT_O * WKEY], F32)
                sc_s = sb7("sc_s", [128, 2 * WKEY], F32)
                pbf = sb7("pbf", [128, 2 * WKEY], BF16)
                ptb = sb7("ptb", [128, 5 * T_OWN], BF16)
                ps_sc = [ph7.enter_context(nc.psum_tensor(f"ps_sc{k}", [128, WKEY], F32))
                         for k in range(2)]
                ps_pv = ph7.enter_context(nc.psum_tensor("ps_pv", [128, 1024], F32))

                @blk.gpsimd
                def _(g):
                    g.dma_start(masks.rearrange("p (n k) -> p n k", n=NT_O), mask_in.rearrange("n p k -> p n k")).then_inc(sG.sem, 16)
                sG.v += 16
                MSK_G = sG.v

                for hq in range(QH):
                    @blk.sync
                    def _(y, hq=hq):
                        y.dma_start_transpose(qf[:, hq * T_OWN:(hq + 1) * T_OWN],
                                              q_hbm[0:T_OWN, hq * 128:(hq + 1) * 128]).then_inc(sY.sem, 16)
                    sY.v += 16
                for gk in range(KVH):
                    @blk.sync
                    def _(y, gk=gk):
                        y.dma_start_transpose(kf[:, gk * T_BLK:(gk + 1) * T_BLK],
                                              k_hbm[0:T_BLK, gk * 128:(gk + 1) * 128]).then_inc(sY.sem, 16)
                    sY.v += 16
                QKF_Y = sY.v

                sc_ev = [sA.v]
                pwg = []
                pv_ev = [sA.v]
                pw_hist = {}
                pr_hist = {}
                pvmm_hist = {}

                def emit_scores(hq):
                    gk = hq // 4
                    hpar = hq % 2
                    for to in range(NT_O):
                        tp = to % 2

                        @blk.tensor
                        def _(t, hq=hq, gk=gk, to=to, tp=tp, ev=sc_ev[0]):
                            if hq == 0 and to == 0:
                                t.wait_ge(sY.sem, QKF_Y)
                                t.wait_ge(sG.sem, MSK_G)
                            if hq * NT_O + to >= 1:
                                t.wait_ge(sA.sem, ev)
                            st = qf[:, hq * T_OWN + to * 128:hq * T_OWN + to * 128 + 128]
                            base = gk * T_BLK + to * 128
                            t.matmul(ps_sc[tp][:, 0:512], st, kf[:, base:base + 512],
                                     start=True, stop=True)
                            t.matmul(ps_sc[tp][:, 512:640], st, kf[:, base + 512:base + 640],
                                     start=True, stop=True).then_inc(sM.sem, 1)
                        sM.v += 1
                        c_m = sM.v

                        @blkV
                        def _(v, to=to, tp=tp, c_m=c_m):
                            v.wait_ge(sM.sem, c_m)
                            v.tensor_tensor(sc_s[:, tp * WKEY:(tp + 1) * WKEY], ps_sc[tp][:],
                                            masks[:, to * WKEY:(to + 1) * WKEY], OP.add)
                            v.tensor_reduce(scal[:, 60:61], sc_s[:, tp * WKEY:(tp + 1) * WKEY],
                                            AX.X, OP.max, negate=True)
                            v.drain().then_inc(sV.sem, 1)
                        sV.v += 1
                        c_v = sV.v

                        @blkS
                        def _(s, tp=tp, c_v=c_v,
                              wg=(pwg[hq * NT_O + to - 1] if hq * NT_O + to >= 1 else 0)):
                            s.wait_ge(sV.sem, c_v)
                            if wg:
                                s.wait_ge(sG.sem, wg)
                            s.activation(pbf[:, tp * WKEY:(tp + 1) * WKEY],
                                         sc_s[:, tp * WKEY:(tp + 1) * WKEY], ACT.Exp,
                                         bias=scal[:, 60:61], scale=1.0,
                                         accum_out=scal[:, 61:62])
                            s.drain().then_inc(sA.sem, 1)
                        sA.v += 1
                        c_a = sA.v

                        @blkV
                        def _(v, c_a=c_a):
                            v.wait_ge(sA.sem, c_a)
                            v.reciprocal(scal[:, 62:63], scal[:, 61:62])
                            v.drain().then_inc(sV.sem, 1)
                        sV.v += 1
                        c_v2 = sV.v

                        @blkS
                        def _(s, tp=tp, c_v2=c_v2):
                            s.wait_ge(sV.sem, c_v2)
                            s.mul(pbf[:, tp * WKEY:(tp + 1) * WKEY],
                                  pbf[:, tp * WKEY:(tp + 1) * WKEY], scal[:, 62:63])
                            s.drain().then_inc(sA.sem, 1)
                        sA.v += 1
                        sc_ev[0] = sA.v
                        c_a2 = sA.v

                        @blk.gpsimd
                        def _(g, hq=hq, hpar=hpar, to=to, tp=tp, c_a2=c_a2,
                              rr=(pr_hist.get(hq - 2, 0) if to == 0 else 0)):
                            g.wait_ge(sA.sem, c_a2)
                            if rr:
                                g.wait_ge(sY.sem, rr)
                            g.dma_start(p_hbm[hpar][to * 128:(to + 1) * 128, :],
                                        pbf[:, tp * WKEY:(tp + 1) * WKEY]).then_inc(sG.sem, 16)
                        sG.v += 16
                        pwg.append(sG.v)
                    pw_hist[hq] = sG.v

                def emit_pv(hq):
                    gk = hq // 4
                    hpar = hq % 2
                    for arel in range(5):
                        @blk.sync
                        def _(y, hpar=hpar, arel=arel, pw=pw_hist[hq],
                              mm=pvmm_hist.get(hq - 1, 0)):
                            y.wait_ge(sG.sem, pw)
                            if mm:
                                y.wait_ge(sM.sem, mm)
                            y.dma_start_transpose(ptb[:, arel * T_OWN:(arel + 1) * T_OWN],
                                                  p_hbm[hpar][0:T_OWN, arel * 128:(arel + 1) * 128]).then_inc(sY.sem, 16)
                        sY.v += 16
                    c_y = sY.v

                    @blk.tensor
                    def _(t, hq=hq, gk=gk, c_y=c_y, ev=pv_ev[0]):
                        t.wait_ge(sY.sem, c_y)
                        t.wait_ge(sA.sem, ev)
                        for to in range(NT_O):
                            for arel in range(5):
                                a_abs = to + arel
                                t.matmul(ps_pv[:, to * 128:(to + 1) * 128],
                                         v_tm[:, a_abs * 512 + gk * 128:a_abs * 512 + gk * 128 + 128],
                                         ptb[:, arel * T_OWN + to * 128:arel * T_OWN + to * 128 + 128],
                                         start=(arel == 0), stop=(arel == 4))
                        t.drain().then_inc(sM.sem, 1)
                    sM.v += 1
                    pvmm_hist[hq] = sM.v
                    c_m = sM.v

                    @blkS
                    def _(s, hq=hq, c_m=c_m):
                        s.wait_ge(sM.sem, c_m)
                        s.activation(aout[:, hq * T_OWN:(hq + 1) * T_OWN], ps_pv[:], ACT.Copy,
                                     bias=0.0, scale=1.0)
                        s.drain().then_inc(sA.sem, 1)
                    sA.v += 1
                    pv_ev[0] = sA.v
                    pr_hist[hq] = sY.v

                emit_scores(0)
                for hq in range(1, QH):
                    emit_scores(hq)
                    emit_pv(hq - 1)
                emit_pv(QH - 1)
                barrier()

            ph8 = contextlib.ExitStack()
            with ph8:
                wom = ph8.enter_context(nc.sbuf_tensor("wom", [128, 16 * D], BF16))
                xts[0] = ph8.enter_context(nc.sbuf_tensor("xt8", [128, 2 * D], F32))
                ps_w = ph8.enter_context(nc.psum_tensor("ps_w", [128, D], F32))

                @blk.gpsimd
                def _(g):
                    for o in range(16):
                        g.dma_start(wom[:, o * D:(o + 1) * D], wom_in[o]).then_inc(sG.sem, 16)
                sG.v += 16 * 16
                WOM_G = sG.v

                w_ev = [sV.v]
                for to in range(NT_O):
                    tc = to + NHALO
                    par = to % 2

                    @blk.gpsimd
                    def _(g, tc=tc, par=par, wv=(w_ev[0] if to >= 1 else 0), wg=sG.v):
                        if wv:
                            g.wait_ge(sV.sem, wv)
                        g.wait_ge(sG.sem, wg)
                        g.dma_start(xts[0][:, par * D:par * D + D],
                                    xw_hbm[tc * 128:(tc + 1) * 128, :]).then_inc(sG.sem, 16)
                    sG.v += 16
                    c_x = sG.v

                    @blk.tensor
                    def _(t, to=to, ev=(w_ev[0] if to >= 1 else 0), WOM_G=WOM_G):
                        if to == 0:
                            t.wait_ge(sG.sem, WOM_G)
                        else:
                            t.wait_ge(sV.sem, ev)
                        for o in range(16):
                            st = aout[:, o * T_OWN + to * 128:o * T_OWN + to * 128 + 128]
                            for dq in range(4):
                                t.matmul(ps_w[:, dq * 512:(dq + 1) * 512], st,
                                         wom[:, o * D + dq * 512:o * D + (dq + 1) * 512],
                                         start=(o == 0), stop=(o == 15))
                        t.drain().then_inc(sM.sem, 1)
                    sM.v += 1
                    c_m = sM.v

                    @blkV
                    def _(v, par=par, c_m=c_m, c_x=c_x):
                        v.wait_ge(sM.sem, c_m)
                        v.wait_ge(sG.sem, c_x)
                        v.tensor_add(xts[0][:, par * D:par * D + D],
                                     xts[0][:, par * D:par * D + D], ps_w[:])
                        v.drain().then_inc(sV.sem, 1)
                    sV.v += 1
                    w_ev[0] = sV.v
                    c_v = sV.v

                    @blk.gpsimd
                    def _(g, tc=tc, par=par, c_v=c_v):
                        g.wait_ge(sV.sem, c_v)
                        g.dma_start(xw_hbm[tc * 128:(tc + 1) * 128, :],
                                    xts[0][:, par * D:par * D + D]).then_inc(sG.sem, 16)
                    sG.v += 16
                barrier()
            aout_cm.__exit__(None, None, None)

        ph9a = contextlib.ExitStack()
        with ph9a:
            tbf9 = ph9a.enter_context(nc.sbuf_tensor("tbf9", [128, D], BF16))
            qi89 = ph9a.enter_context(nc.sbuf_tensor("qi89", [128, D], I8))
            xts[0] = ph9a.enter_context(nc.sbuf_tensor("xt9", [128, 2 * D], F32))
            lu9 = {0: sV.v, 1: sV.v}
            for i in range(NT_O):
                row = i + NHALO
                par = i % 2

                @blk.gpsimd
                def _(g, row=row, par=par, lu=lu9[par]):
                    g.wait_ge(sV.sem, lu)
                    g.dma_start(xts[0][:, par * D:par * D + D],
                                xw_hbm[row * 128:(row + 1) * 128, :]).then_inc(sG.sem, 16)
                sG.v += 16
                c_g = sG.v

                @blkV
                def _(v, par=par, c_g=c_g):
                    v.wait_ge(sG.sem, c_g)
                    xv = xts[0][:, par * D:par * D + D]
                    v.tensor_reduce(scal[:, 60:61], xv, AX.X, OP.add)
                    v.tensor_scalar(scal[:, 60:61], scal[:, 60:61], 1.0 / D, None, OP.mult)
                    v.tensor_scalar(scal[:, 61:62], scal[:, 60:61], -1.0, None, OP.mult)
                    v.drain().then_inc(sV.sem, 1)
                sV.v += 1
                c_v1 = sV.v

                @blkS
                def _(s, par=par, c_v1=c_v1):
                    s.wait_ge(sV.sem, c_v1)
                    xv = xts[0][:, par * D:par * D + D]
                    s.activation(tmp1[:], xv, ACT.Square, bias=scal[:, 61:62], scale=1.0,
                                 accum_out=scal[:, 62:63])
                    s.drain().then_inc(sA.sem, 1)
                sA.v += 1
                c_a1 = sA.v

                @blkV
                def _(v, c_a1=c_a1):
                    v.wait_ge(sA.sem, c_a1)
                    v.tensor_reduce(scal[:, 63:64], tmp1[:], AX.X, OP.max)
                    v.drain().then_inc(sV.sem, 1)
                sV.v += 1
                c_v2 = sV.v

                @blkS
                def _(s, c_v2=c_v2):
                    s.wait_ge(sV.sem, c_v2)
                    s.activation(scal[:, 55:56], scal[:, 62:63], ACT.Sqrt, bias=scal[:, 15:16],
                                 scale=1.0 / D)
                    s.activation(scal[:, 57:58], scal[:, 63:64], ACT.Sqrt, bias=0.0, scale=1.0)
                    s.drain().then_inc(sA.sem, 1)
                sA.v += 1
                c_a2 = sA.v

                @blkV
                def _(v, i=i, par=par, c_a2=c_a2):
                    v.wait_ge(sA.sem, c_a2)
                    xv = xts[0][:, par * D:par * D + D]
                    v.reciprocal(scal[:, 56:57], scal[:, 55:56])
                    v.tensor_mul(scal[:, 58:59], scal[:, 57:58], scal[:, 56:57])
                    v.tensor_scalar(scal[:, 58:59], scal[:, 58:59], 1e-5, None, OP.max)
                    v.reciprocal(scal[:, 59:60], scal[:, 58:59])
                    v.tensor_scalar(scal[:, 59:60], scal[:, 59:60], 127.0, None, OP.mult)
                    v.tensor_mul(scal[:, 59:60], scal[:, 59:60], scal[:, 56:57])
                    v.tensor_scalar(tmp1[:], xv, scal[:, 60:61], scal[:, 59:60],
                                    OP.subtract, OP.mult)
                    v.tensor_copy(qi89[:], tmp1[:])
                    v.tensor_mul(scal[:, 16 + i:17 + i], scal[:, 58:59], scal[:, 52:53])
                    v.tensor_mul(scal[:, 28 + i:29 + i], scal[:, 58:59], scal[:, 53:54])
                    v.reciprocal(scal[:, 40 + i:41 + i], scal[:, 58:59])
                    v.drain().then_inc(sV.sem, 1)
                sV.v += 1
                lu9[par] = sV.v
                c_v3 = sV.v

                @blkS
                def _(s, c_v3=c_v3, wg=sG.v):
                    s.wait_ge(sV.sem, c_v3)
                    s.wait_ge(sG.sem, wg)
                    s.activation(tbf9[:], qi89[:], ACT.Copy, bias=0.0, scale=1.0)
                    s.drain().then_inc(sA.sem, 1)
                sA.v += 1
                c_a3 = sA.v

                @blk.gpsimd
                def _(g, i=i, c_a3=c_a3):
                    g.wait_ge(sA.sem, c_a3)
                    g.dma_start(q1_hbm[i * 128:(i + 1) * 128, :], tbf9[:]).then_inc(sG.sem, 16)
                sG.v += 16

            for i in range(NT_O):
                @blk.gpsimd
                def _(g, i=i, pv=sV.v):
                    g.wait_ge(sV.sem, pv)
                    g.dma_start(ar_hbm[i:i + 1, :], scal[:, 40 + i:41 + i]).then_inc(sG.sem, 16)
                sG.v += 16
                cw = sG.v

                @blk.gpsimd
                def _(g, i=i, cw=cw):
                    g.wait_ge(sG.sem, cw)
                    g.dma_start(arow_s2[0:1, :], ar_hbm[i:i + 1, :]).then_inc(sG.sem, 16)
                sG.v += 16
                cr = sG.v

                @blkV
                def _(v, i=i, cr=cr):
                    v.wait_ge(sG.sem, cr)
                    v.tensor_scalar(grow[0:1, i * 128:(i + 1) * 128], arow_s2[0:1, :],
                                    gam[0:1, 6:7], None, OP.mult)
                    v.tensor_scalar(vrow[0:1, i * 128:(i + 1) * 128], arow_s2[0:1, :],
                                    gam[0:1, 7:8], None, OP.mult)
                    v.drain().then_inc(sV.sem, 1)
                sV.v += 1
            barrier()

        ph9 = contextlib.ExitStack()
        with ph9:
            emit_ffn_gv(NT_O, int8src=True, ph=ph9, tag='B')
            barrier()

        ph10 = contextlib.ExitStack()
        with ph10:
            emit_ffn3(NT_O, int8src=True, ph=ph10,
                      x_src=xw_hbm[NHALO * 128:, :], x_dst=out_hbm, tag='B')
            fin = sG.v

            @blk.gpsimd
            def _(g, fin=fin):
                g.wait_ge(sG.sem, fin)

        blk_cm.__exit__(None, None, None)
    return nc


_CACHED = {}


def _shared_prep(inputs):
    gw = np.asarray(inputs["L0_ffn_gate_w"])
    vw = np.asarray(inputs["L0_ffn_val_w"])
    ow = np.asarray(inputs["L0_ffn_out_w"])
    wq = np.asarray(inputs["L0_attn_wq"])
    wk = np.asarray(inputs["L0_attn_wk"])
    wv = np.asarray(inputs["L0_attn_wv"])
    wo = np.asarray(inputs["L0_attn_wo"])
    C = {}
    C["wg"] = np.ascontiguousarray(gw.T.reshape(D, NHT, 512).transpose(1, 0, 2))
    C["wv"] = np.ascontiguousarray(vw.T.reshape(D, NHT, 512).transpose(1, 0, 2))
    C["wo3"] = np.ascontiguousarray(ow.T.reshape(NHC, 128, 4, 512).transpose(2, 0, 1, 3))
    C["wq"] = np.ascontiguousarray(wq.T.reshape(NDC, 128, D))
    C["wk"] = np.ascontiguousarray(wk.T.reshape(NDC, 128, 512))
    C["wvv"] = np.ascontiguousarray(wv.T.reshape(NDC, 128, 512))
    C["wom"] = np.ascontiguousarray(wo.T.reshape(16, 128, D))
    C["gb"] = np.ascontiguousarray(np.asarray(inputs["L0_ffn_gate_b"]).reshape(1, H))
    C["vb"] = np.ascontiguousarray(np.asarray(inputs["L0_ffn_val_b"]).reshape(1, H))
    C["ob"] = np.ascontiguousarray(np.asarray(inputs["L0_ffn_out_b"]).reshape(1, D))
    C["ident"] = np.eye(128, dtype=BF)
    C["ones"] = np.ones((1, 128), np.float32)
    half = 64
    inv = (1.0 / 10000.0 ** (np.arange(half, dtype=np.float32) / half)).astype(np.float32)
    t_all = np.arange(8192, dtype=np.float32)
    fr = np.outer(t_all, inv)
    C["cos"] = np.cos(fr).astype(np.float32)
    C["sin"] = np.sin(fr).astype(np.float32)
    return C


def kernel(**inputs):
    C = _shared_prep(inputs)
    nc = build_program()
    scale = np.float32(1.0 / np.sqrt(128.0))
    x = np.asarray(inputs["L0_attn_x"]).reshape(8192, 2048)
    in_maps = []
    for core in range(NCORES):
        lo = core * T_OWN - HALO
        if lo < 0:
            xb = np.ascontiguousarray(np.concatenate([x[0:HALO], x[0:T_OWN]], axis=0))
        else:
            xb = np.ascontiguousarray(x[lo:core * T_OWN + T_OWN])
        pos_blk = np.clip(np.arange(lo, lo + T_BLK), 0, 8191)
        pos_own = np.arange(core * T_OWN, (core + 1) * T_OWN)
        ctq = np.ascontiguousarray(np.tile(C["cos"][pos_own] * scale, (1, 8)).astype(np.float32))
        stq = np.ascontiguousarray(np.tile(C["sin"][pos_own] * scale, (1, 8)).astype(np.float32))
        ctk = np.ascontiguousarray(np.tile(C["cos"][pos_blk], (1, 4)).astype(np.float32))
        stk = np.ascontiguousarray(np.tile(C["sin"][pos_blk], (1, 4)).astype(np.float32))
        to_ = np.arange(NT_O)[:, None, None]
        ii = np.arange(128)[None, :, None]
        jj = np.arange(WKEY)[None, None, :]
        qpos = core * T_OWN + to_ * 128 + ii
        kpos = lo + to_ * 128 + jj
        ok = (qpos - kpos >= 0) & (qpos - kpos <= WIN) & (kpos >= 0)
        mask = np.where(ok, np.float32(0.0), np.float32(-1e30)).astype(np.float32)
        in_maps.append({
            "x_in": xb,
            "wg_in": C["wg"], "wv_in": C["wv"], "wo_in": C["wo3"],
            "wq_in": C["wq"], "wk_in": C["wk"], "wvv_in": C["wvv"], "wom_in": C["wom"],
            "gb_in": C["gb"], "vb_in": C["vb"], "ob_in": C["ob"],
            "mask_in": np.ascontiguousarray(mask),
            "ctq_in": ctq, "stq_in": stq, "ctk_in": ctk, "stk_in": stk,
            "ident_in": C["ident"], "ones_in": C["ones"],
        })
    res = run_bass_kernel_spmd(nc, in_maps, core_ids=list(range(NCORES)))
    out = np.concatenate([np.asarray(r["out"]) for r in res.results], axis=0)
    return out.reshape(1, 8192, 2048).astype(np.float32)
